# version 23 (final)
# speedup vs baseline: 1.0324x; 1.0002x over previous
import numpy as np
import ml_dtypes
import concourse.bass as bass
import concourse.mybir as mybir
from concourse.alu_op_type import AluOpType as OP
from concourse.bass_utils import run_bass_kernel_spmd
from contextlib import ExitStack

F32 = mybir.dt.float32
BF16 = mybir.dt.bfloat16
AF = mybir.ActivationFunctionType
AX = mybir.AxisListType

NB = 2
T = 2048
C = 1024
NT = 16
NCT = 8
NH = 16
HD = 64
HF = 8
EF = 5
EPS = 1e-5

DEV_TAPS = ()
DEV_PHASES = "all"
DEV_NB = None
DEV_PAIRS = 8
DEV_HEADS = 2
DEV_FHEADS = 8
DEV_SMAX = 16
DEV_NOACC = False
DEV_NOCROSS = False
DEV_FSTAGE = 4
DEV_ZSUB = 3

ENGINES = ("tensor", "vector", "scalar", "gpsimd", "sync")
CSEM = {"tensor": "tP", "vector": "tV", "scalar": "tS", "gpsimd": "tG", "sync": "tY"}
NDMA = 8
DSEMS = {"sync": [f"dS{i}" for i in range(NDMA)],
         "gpsimd": [f"dG{i}" for i in range(NDMA)]}
ALL_SEMS = list(CSEM.values()) + DSEMS["sync"] + DSEMS["gpsimd"]


class Tok:
    __slots__ = ("sem", "eng", "idx", "val")

    def __init__(self, sem, eng, idx):
        self.sem = sem
        self.eng = eng
        self.idx = idx
        self.val = None


class Prog:
    def __init__(self):
        self.eops = {e: [] for e in ENGINES}
        self.last = {e: None for e in ENGINES}
        self.dma_i = {e: 0 for e in DSEMS}
        self.dma_last = {s: None for ss in DSEMS.values() for s in ss}

    def emit(self, eng, fn, waits=(), dma=False):
        waits = [w for w in waits if w is not None]
        if dma:
            sem = DSEMS[eng][self.dma_i[eng] % NDMA]
            self.dma_i[eng] += 1
            if self.dma_last[sem] is not None:
                waits = waits + [self.dma_last[sem]]
        else:
            sem = CSEM[eng]
        tok = Tok(sem, eng, len(self.eops[eng]))
        self.eops[eng].append([fn, tuple(waits), tok, 16 if dma else 1])
        self.last[eng] = tok
        if dma:
            self.dma_last[sem] = tok
        return tok

    def barrier(self, extra=()):
        toks = [self.last[e] for e in ENGINES if self.last[e] is not None]
        toks += [t for t in self.dma_last.values() if t is not None]
        toks += [t for t in extra if t is not None]
        for e in ENGINES:
            self.emit(e, None, waits=toks)

    def finalize(self):
        used = set()
        for e in ENGINES:
            for op in self.eops[e]:
                for w in op[1]:
                    used.add(w)
        counts = {}
        for e in ENGINES:
            for op in self.eops[e]:
                tok = op[2]
                if tok in used or op[3] == 16:
                    counts[tok.sem] = counts.get(tok.sem, 0) + op[3]
                    tok.val = counts[tok.sem]
                else:
                    op[2] = None

    def replay(self, eng_name, eng, sems):
        done = {}
        for fn, waits, tok, inc_by in self.eops[eng_name]:
            for w in waits:
                assert w.val is not None
                if done.get(w.sem, 0) < w.val:
                    eng.wait_ge(sems[w.sem], w.val)
                    done[w.sem] = w.val
            instr = fn(eng) if fn is not None else None
            if tok is not None:
                if instr is None:
                    instr = eng.nop()
                instr.then_inc(sems[tok.sem], inc_by)


def make_consts():
    c = {}
    ii = np.eye(128, dtype=np.float32)
    c["c_identf"] = ii
    c["c_identb"] = ii.astype(ml_dtypes.bfloat16)
    e2 = np.zeros((2, 128), np.float32)
    e2[0, 0:64] = 1.0
    e2[1, 64:128] = 1.0
    c["c_e2f"] = e2
    c["c_e2b"] = e2.astype(ml_dtypes.bfloat16)
    gamma = 1.0 - 2.0 ** (-5.0 - np.arange(NH, dtype=np.float64))
    tt = np.arange(T, dtype=np.float64)
    c["c_gammarows"] = np.power(gamma[:, None], tt[None, :]).astype(ml_dtypes.bfloat16)
    p = np.arange(128, dtype=np.float64)
    gk = np.zeros((128, NH * 16), np.float64)
    for h in range(NH):
        for s in range(16):
            gk[:, h * 16 + s] = np.power(gamma[h], -(s * 128.0 + p))
    c["c_gkcols"] = gk.astype(np.float32)
    x = np.arange(512)
    c["c_dmasks"] = (x[None, :] >= p[:, None]).astype(ml_dtypes.bfloat16)
    c["c_onescol_b"] = np.ones((128, 1), ml_dtypes.bfloat16)
    c["c_onescol_f"] = np.ones((128, 1), np.float32)
    c["c_ones1f"] = np.ones((1, 128), np.float32)
    cc = np.zeros((128, 8), np.float32)
    cc[:, 0] = EPS
    cc[:, 1] = 1.0
    c["c_cols"] = cc
    sc = np.zeros((128, 128), np.float64)
    sc[:, 0] = 1.0
    c["c_sumcol"] = sc.astype(ml_dtypes.bfloat16)
    return c


class Ctx:
    pass


def build(taps=()):
    nc = bass.Bass("TRN2", target_bir_lowering=False)
    pr = Prog()
    cx = Ctx()
    cx.nc, cx.pr = nc, pr
    cx.taps = dict((t[0], None) for t in taps)

    d = {}
    d["x"] = nc.dram_tensor("x", [NB, T, C], F32, kind="ExternalInput").ap()
    d["Wqkv"] = nc.dram_tensor("Wqkv", [C, 3 * C], F32, kind="ExternalInput").ap()
    d["pkv"] = nc.dram_tensor("pkv", [8, 128, HD], F32, kind="ExternalInput").ap()
    d["gnw2"] = nc.dram_tensor("gnw2", [2, 8], F32, kind="ExternalInput").ap()
    d["gnb2"] = nc.dram_tensor("gnb2", [2, 8], F32, kind="ExternalInput").ap()
    d["Wq"] = nc.dram_tensor("Wq", [C, 2560], F32, kind="ExternalInput").ap()
    d["Wkv"] = nc.dram_tensor("Wkv", [C, 5120], F32, kind="ExternalInput").ap()
    d["Wswd"] = nc.dram_tensor("Wswd", [C, 80], F32, kind="ExternalInput").ap()
    d["Wo_hat"] = nc.dram_tensor("Wo_hat", [640, 1024], F32, kind="ExternalInput").ap()
    for k, v in make_consts().items():
        dt = BF16 if v.dtype == ml_dtypes.bfloat16 else F32
        d[k] = nc.dram_tensor(k, list(v.shape), dt, kind="ExternalInput").ap()
    d["y"] = nc.dram_tensor("y", [NB, T, C], F32, kind="ExternalOutput").ap()
    for tname, shape, dt in taps:
        cx.taps[tname] = nc.dram_tensor(tname, shape, dt, kind="ExternalOutput").ap()
    cx.d = d
    cx.tapspecs = taps
    cx.store_toks = []

    with ExitStack() as top:
        e = top.enter_context
        cx.e = e
        cx.xT = e(nc.sbuf_tensor("xT", [128, NCT * T], F32))
        cx.xnT = e(nc.sbuf_tensor("xnT", [128, NCT * T], BF16))
        cx.identf = e(nc.sbuf_tensor("identf", [128, 128], F32))
        cx.identb = e(nc.sbuf_tensor("identb", [128, 128], BF16))
        cx.e2f = e(nc.sbuf_tensor("e2f", [2, 128], F32))
        cx.e2b = e(nc.sbuf_tensor("e2b", [2, 128], BF16))
        cx.gkcols = e(nc.sbuf_tensor("gkcols", [128, NH * 16], F32))
        cx.dmasks = e(nc.sbuf_tensor("dmasks", [128, 512], BF16))
        cx.onescol_b = e(nc.sbuf_tensor("onescol_b", [128, 1], BF16))
        cx.sumcol = e(nc.sbuf_tensor("sumcol", [128, 128], BF16))
        cx.onescol_f = e(nc.sbuf_tensor("onescol_f", [128, 1], F32))
        cx.ones1f = e(nc.sbuf_tensor("ones1f", [1, 128], F32))
        cx.ccols = e(nc.sbuf_tensor("ccols", [128, 8], F32))
        cx.pkvb = e(nc.sbuf_tensor("pkvb", [128, 8 * HD], BF16))
        cx.gnw2s = e(nc.sbuf_tensor("gnw2s", [2, 8], F32))
        cx.gnb2s = e(nc.sbuf_tensor("gnb2s", [2, 8], F32))
        cx.wo_hat = e(nc.sbuf_tensor("wo_hat", [128, 5 * 1024], BF16))
        cx.zp = e(nc.sbuf_tensor("zp", [128, 5 * T], BF16))
        cx.PA = e(nc.psum_tensor("PA", [128, 2048], F32))
        cx.PB = e(nc.psum_tensor("PB", [128, 1024], F32))
        cx.PBF = e(nc.psum_tensor("PBF", [128, 2048], BF16))
        sems = {}
        for s in ALL_SEMS:
            sems[s] = e(nc.semaphore(s))
        cx.sems = sems

        emit_const_loads(cx)
        pr.barrier()
        prev_store = None
        for b in range(DEV_NB if DEV_NB is not None else NB):
            emit_batch(cx, b, prev_store)
            prev_store = cx.store_toks[-1] if cx.store_toks else None
            pr.barrier(extra=[prev_store])
        if cx.store_toks:
            pr.emit("sync", None, waits=[cx.store_toks[-1]])
        pr.finalize()

        with nc.Block() as block:
            for ename in ENGINES:
                def mk(en):
                    return lambda eng: pr.replay(en, eng, sems)
                getattr(block, ename)(mk(ename))
    return nc


def emit_const_loads(cx):
    pr, d = cx.pr, cx.d
    nc = cx.nc
    pairs = [
        (cx.identf, "c_identf"), (cx.identb, "c_identb"),
        (cx.e2f, "c_e2f"), (cx.e2b, "c_e2b"),
        (cx.gkcols, "c_gkcols"), (cx.dmasks, "c_dmasks"),
        (cx.onescol_b, "c_onescol_b"), (cx.onescol_f, "c_onescol_f"),
        (cx.sumcol, "c_sumcol"),
        (cx.ones1f, "c_ones1f"), (cx.ccols, "c_cols"),
        (cx.gnw2s, "gnw2"), (cx.gnb2s, "gnb2"),
    ]
    for sb, key in pairs:
        pr.emit("sync", (lambda s_, k_: lambda s: s.dma_start(out=s_[:, :], in_=d[k_]))(sb, key),
                dma=True)
    tok_pkvf = pr.emit("gpsimd", lambda g: g.dma_start(
        out=cx.pkvb.ap().rearrange("p (a dd) -> p a dd", a=8),
        in_=d["pkv"].rearrange("a p dd -> p a dd")), dma=True)
    cx.tok_wo = pr.emit("gpsimd", lambda g: g.dma_start(
        out=cx.wo_hat.ap().rearrange("p (a m) -> p a m", a=5),
        in_=d["Wo_hat"].rearrange("(a p) m -> p a m", p=128)), dma=True)
    cx.tok_pkvb = pr.emit("vector", lambda v: v.tensor_scalar(
        out=cx.pkvb[:, :], in0=cx.pkvb[:, :], scalar1=1024.0, scalar2=None,
        op0=OP.mult), waits=[tok_pkvf])


def tap_on(cx, b, name):
    return name in cx.taps and cx.taps[name] is not None and b == 0


def emit_batch(cx, b, prev_store):
    nc, pr, d = cx.nc, cx.pr, cx.d
    xT, PB, identf = cx.xT, cx.PB, cx.identf

    with ExitStack() as ph:
        pe = ph.enter_context
        xstage = [pe(nc.sbuf_tensor(f"xs{b}{i}", [128, C], F32)) for i in range(2)]
        last_tp_on_stage = [None, None]
        last_ev_on_bank = [None, None]
        ld = [None, None]
        for t in range(NT):
            par = t % 2
            w = [last_tp_on_stage[par], prev_store]
            ld[par] = pr.emit("sync", (lambda st, pa: lambda s: s.dma_start(
                out=xstage[pa][:, :], in_=d["x"][b, st * 128:(st + 1) * 128, :]))(t, par),
                waits=w, dma=True)
            for tau in range(NCT):
                bank = tau % 2
                tp = pr.emit("tensor", (lambda pa, ta, bk: lambda te: te.transpose(
                    PB[:, bk * 512:bk * 512 + 128],
                    xstage[pa][:, ta * 128:(ta + 1) * 128], identf[:, :]))(par, tau, bank),
                    waits=[ld[par], last_ev_on_bank[bank]])
                ev = pr.emit("vector", (lambda ta, st, bk: lambda v: v.tensor_copy(
                    out=xT[:, ta * T + st * 128:ta * T + (st + 1) * 128],
                    in_=PB[:, bk * 512:bk * 512 + 128]))(tau, t, bank), waits=[tp])
                last_ev_on_bank[bank] = ev
                last_tp_on_stage[par] = tp
    pr.barrier()
    if DEV_PHASES == "in":
        return

    ln_done = emit_ln(cx, b, "ln1")
    if tap_on(cx, b, "t_xn1"):
        cx.store_toks.append(pr.emit("sync", lambda s: s.dma_start(
            out=cx.taps["t_xn1"], in_=cx.xnT[:, :]), waits=[ln_done], dma=True))
    pr.barrier()
    if DEV_PHASES == "ln1":
        return

    emit_retention(cx, b, ln_done)
    if tap_on(cx, b, "t_x1"):
        cx.store_toks.append(pr.emit("sync", lambda s: s.dma_start(
            out=cx.taps["t_x1"], in_=cx.xT[:, :]), waits=[pr.last["vector"]], dma=True))
    pr.barrier()
    if DEV_PHASES == "ret":
        return

    cx.lncols = getattr(cx, "lncols", None)
    if cx.lncols is None:
        cx.lncols = cx.e(nc.sbuf_tensor("lncols", [128, 48], F32))
    ln2_done = emit_ln(cx, b, "ln2", out_cols=cx.lncols)
    if tap_on(cx, b, "t_xn2"):
        cx.store_toks.append(pr.emit("sync", lambda s: s.dma_start(
            out=cx.taps["t_xn2"], in_=cx.xnT[:, :]), waits=[ln2_done], dma=True))
    pr.barrier()
    if DEV_PHASES == "ln2":
        return

    emit_ffn(cx, b, ln2_done)
    pr.barrier()
    if DEV_PHASES == "ffn":
        return

    with ExitStack() as ph:
        pe = ph.enter_context
        ystage = [pe(nc.sbuf_tensor(f"ys{b}{i}", [128, C], F32)) for i in range(2)]
        st_by_t = {}
        last_ev_on_bank = [None, None]
        for t in range(NT):
            par = t % 2
            evs = []
            for tau in range(NCT):
                bank = tau % 2
                tp = pr.emit("tensor", (lambda ta, tt, bk: lambda te: te.transpose(
                    PB[:, bk * 512:bk * 512 + 128],
                    xT[:, ta * T + tt * 128:ta * T + (tt + 1) * 128], identf[:, :]))(tau, t, bank),
                    waits=[last_ev_on_bank[bank]])
                ev = pr.emit("vector", (lambda ta, pa, bk: lambda v: v.tensor_copy(
                    out=ystage[pa][:, ta * 128:(ta + 1) * 128],
                    in_=PB[:, bk * 512:bk * 512 + 128]))(tau, par, bank),
                    waits=[tp, st_by_t.get(t - 2)])
                last_ev_on_bank[bank] = ev
                evs.append(ev)
            st = pr.emit("sync", (lambda tt, pa: lambda s: s.dma_start(
                out=d["y"][b, tt * 128:(tt + 1) * 128, :], in_=ystage[pa][:, :]))(t, par),
                waits=evs, dma=True)
            st_by_t[t] = st
            cx.store_toks.append(st)


def emit_ln(cx, b, name, out_cols=None):
    nc, pr = cx.nc, cx.pr
    xT, xnT, PB = cx.xT, cx.xnT, cx.PB
    onescol_b, ones1f, ccols = cx.onescol_b, cx.ones1f, cx.ccols
    with ExitStack() as ph:
        pe = ph.enter_context
        xs = [pe(nc.sbuf_tensor(f"{name}{b}xs{i}", [128, 512], BF16)) for i in range(2)]
        sq = [pe(nc.sbuf_tensor(f"{name}{b}sq{i}", [128, 512], BF16)) for i in range(2)]
        murow = pe(nc.sbuf_tensor(f"{name}{b}mu", [1, T], F32))
        vrow = pe(nc.sbuf_tensor(f"{name}{b}vr", [1, T], F32))
        irow = pe(nc.sbuf_tensor(f"{name}{b}ir", [1, T], F32))
        muB = pe(nc.sbuf_tensor(f"{name}{b}muB", [128, T], F32))
        invB = pe(nc.sbuf_tensor(f"{name}{b}invB", [128, T], F32))
        ltmp = pe(nc.sbuf_tensor(f"{name}{b}tmp", [128, 2 * T], F32))
        mm_at = {}
        ev_toks = []
        for tb in range(4):
            mm_x = mm_s = None
            for c in range(NCT):
                par = c % 2
                wl = []
                if c >= 2:
                    wl += [mm_at[(tb, c - 2, 0)], mm_at[(tb, c - 2, 1)]]
                elif tb >= 1:
                    wl += [mm_at[(tb - 1, 6 + c, 0)], mm_at[(tb - 1, 6 + c, 1)]]
                ct = pr.emit("scalar", (lambda c2, t2, pa: lambda sc: sc.copy(
                    out=xs[pa][:, :],
                    in_=xT[:, c2 * T + t2 * 512:c2 * T + (t2 + 1) * 512]))(c, tb, par),
                    waits=wl)
                sqt = pr.emit("scalar", (lambda pa: lambda sc: sc.activation(
                    out=sq[pa][:, :], in_=xs[pa][:, :], func=AF.Square))(par), waits=[ct])
                w_mm = [ct] + (ev_toks[-2:] if (tb > 0 and c == 0) else [])
                mm_x = pr.emit("tensor", (lambda pa, fi, la: lambda te: te.matmul(
                    PB[:, 0:512], cx.sumcol[:, :], xs[pa][:, :],
                    start=fi, stop=la))(par, c == 0, c == NCT - 1), waits=w_mm)
                mm_s = pr.emit("tensor", (lambda pa, fi, la: lambda te: te.matmul(
                    PB[:, 512:1024], cx.sumcol[:, :], sq[pa][:, :],
                    start=fi, stop=la))(par, c == 0, c == NCT - 1), waits=[sqt])
                mm_at[(tb, c, 0)] = mm_x
                mm_at[(tb, c, 1)] = mm_s
            ev1 = pr.emit("vector", (lambda t2: lambda v: v.tensor_scalar(
                out=murow[0:1, t2 * 512:(t2 + 1) * 512], in0=PB[0:1, 0:512],
                scalar1=1.0 / C, scalar2=None, op0=OP.mult))(tb), waits=[mm_x, mm_s])
            ev2 = pr.emit("vector", (lambda t2: lambda v: v.tensor_scalar(
                out=vrow[0:1, t2 * 512:(t2 + 1) * 512], in0=PB[0:1, 512:1024],
                scalar1=1.0 / C, scalar2=None, op0=OP.mult))(tb), waits=[mm_x, mm_s])
            ev_toks += [ev1, ev2]
        mu2 = pr.emit("vector", lambda v: v.tensor_tensor(
            out=irow[0:1, :], in0=murow[0:1, :], in1=murow[0:1, :], op=OP.mult),
            waits=ev_toks)
        var = pr.emit("vector", lambda v: v.tensor_tensor(
            out=vrow[0:1, :], in0=vrow[0:1, :], in1=irow[0:1, :], op=OP.subtract),
            waits=[mu2])
        lnv = pr.emit("scalar", lambda sc: sc.activation(
            out=irow[0:1, :], in_=vrow[0:1, :], func=AF.Ln, bias=ccols[0:1, 0:1],
            scale=1.0), waits=[var])
        inv = pr.emit("scalar", lambda sc: sc.activation(
            out=irow[0:1, :], in_=irow[0:1, :], func=AF.Exp, scale=-0.5), waits=[lnv])
        prev_ev = {0: ev_toks[-1], 1: ev_toks[-1]}
        bc_done = []
        for ch in range(4):
            m1 = pr.emit("tensor", (lambda c2: lambda te: te.matmul(
                PB[:, 0:512], ones1f[:, :], murow[0:1, c2 * 512:(c2 + 1) * 512],
                start=True, stop=True))(ch), waits=[var, prev_ev[0]])
            e1 = pr.emit("vector", (lambda c2: lambda v: v.tensor_copy(
                out=muB[:, c2 * 512:(c2 + 1) * 512], in_=PB[:, 0:512]))(ch), waits=[m1])
            prev_ev[0] = e1
            m2 = pr.emit("tensor", (lambda c2: lambda te: te.matmul(
                PB[:, 512:1024], ones1f[:, :], irow[0:1, c2 * 512:(c2 + 1) * 512],
                start=True, stop=True))(ch), waits=[inv, prev_ev[1]])
            e2_ = pr.emit("vector", (lambda c2: lambda v: v.tensor_copy(
                out=invB[:, c2 * 512:(c2 + 1) * 512], in_=PB[:, 512:1024]))(ch), waits=[m2])
            prev_ev[1] = e2_
            bc_done += [e1, e2_]
        col_tok = None
        if out_cols is not None:
            cms = []
            for t in range(NT):
                cms.append(pr.emit("tensor", (lambda tt: lambda te: te.matmul(
                    PB[:, tt:tt + 1], murow[0:1, tt * 128:(tt + 1) * 128],
                    ccols[0:1, 1:2], start=True, stop=True))(t),
                    waits=[inv, bc_done[-1], bc_done[-2]] if t == 0 else []))
            for t in range(NT):
                cms.append(pr.emit("tensor", (lambda tt: lambda te: te.matmul(
                    PB[:, 16 + tt:17 + tt], irow[0:1, tt * 128:(tt + 1) * 128],
                    ccols[0:1, 1:2], start=True, stop=True))(t)))
            cev = pr.emit("vector", lambda v: v.tensor_copy(
                out=out_cols[:, 0:32], in_=PB[:, 0:32]), waits=[cms[-1]])
            col_tok = pr.emit("vector", lambda v: v.tensor_tensor(
                out=out_cols[:, 32:48], in0=out_cols[:, 0:16],
                in1=out_cols[:, 16:32], op=OP.mult), waits=[cev])
            cx.lncols_tok = col_tok
        out_toks = []
        for tau in range(NCT):
            sl = tau % 2
            w1 = bc_done if tau < 2 else [out_toks[tau - 2]]
            if tau < 2 and col_tok is not None:
                w1 = list(bc_done) + [col_tok]
            s1 = pr.emit("vector", (lambda ta, s2: lambda v: v.tensor_tensor(
                out=ltmp[:, s2 * T:(s2 + 1) * T], in0=xT[:, ta * T:(ta + 1) * T],
                in1=muB[:, :], op=OP.subtract))(tau, sl), waits=w1)
            s2_ = pr.emit("vector", (lambda ta, s2: lambda v: v.tensor_tensor(
                out=xnT[:, ta * T:(ta + 1) * T], in0=ltmp[:, s2 * T:(s2 + 1) * T],
                in1=invB[:, :], op=OP.mult))(tau, sl), waits=[s1])
            out_toks.append(s2_)
        return out_toks[-1]


def emit_retention(cx, b, ln_done):
    nc, pr, d = cx.nc, cx.pr, cx.d
    xT, xnT = cx.xT, cx.xnT
    PA, PB, PBF = cx.PA, cx.PB, cx.PBF
    identb = cx.identb
    e2b, gkcols, dmasks = cx.e2b, cx.gkcols, cx.dmasks
    pkvb = cx.pkvb
    with ExitStack() as ph:
        pe = ph.enter_context
        rT = pe(nc.sbuf_tensor(f"rT{b}", [128, NCT * T], BF16))
        qtp = pe(nc.sbuf_tensor(f"qtp{b}", [128, T], BF16))
        qTp = pe(nc.sbuf_tensor(f"qTp{b}", [128, T], BF16))
        kTp = pe(nc.sbuf_tensor(f"kTp{b}", [128, T], BF16))
        gqB = pe(nc.sbuf_tensor(f"gqB{b}", [128, T], BF16))
        vtok = pe(nc.sbuf_tensor(f"vtok{b}", [128, T], BF16))
        ktokS = pe(nc.sbuf_tensor(f"kts{b}", [128, T], BF16))
        st_ch = pe(nc.sbuf_tensor(f"stc{b}", [128, 4 * 128], BF16))
        Msb = pe(nc.sbuf_tensor(f"Msb{b}", [128, 64], F32))
        Mbf = pe(nc.sbuf_tensor(f"Mbf{b}", [128, 64], BF16))
        dump = gqB
        stats = pe(nc.sbuf_tensor(f"stats{b}", [128, 32], F32))
        sm8 = pe(nc.sbuf_tensor(f"sm8{b}", [128, 40], F32))
        rows = pe(nc.sbuf_tensor(f"rows{b}", [1, 8], F32))
        gam2 = pe(nc.sbuf_tensor(f"gam2{b}", [2, T], BF16))
        wq_t = [pe(nc.sbuf_tensor(f"wqt{b}{i}", [128, C], BF16)) for i in range(2)]
        wk_t = [pe(nc.sbuf_tensor(f"wkt{b}{i}", [128, C], BF16)) for i in range(2)]
        wv_t = [pe(nc.sbuf_tensor(f"wvt{b}{i}", [128, C], BF16)) for i in range(2)]

        def load_gam(p, wdeps):
            return pr.emit("sync", (lambda pp: lambda s: s.dma_start(
                out=gam2[:, :], in_=d["c_gammarows"][2 * pp:2 * pp + 2, :]))(p),
                waits=wdeps, dma=True)

        def load_pair_w(p, wdeps):
            toks = [None]
            for wt, coff in ((wq_t, 0), (wk_t, C), (wv_t, 2 * C)):
                toks.append(pr.emit("gpsimd", (lambda w_, co, pp: lambda g: g.dma_start(
                    out=w_[pp % 2].ap().rearrange("p (a j) -> p a j", a=8),
                    in_=d["Wqkv"].rearrange("(a p) j -> p a j", p=128)[
                        :, :, co + pp * 128:co + (pp + 1) * 128]))(wt, coff, p),
                    waits=wdeps, dma=True))
            return toks

        wtoks = {0: load_pair_w(0, [])}
        gam_tok = {0: load_gam(0, [])}
        stats0 = pr.emit("vector", lambda v: v.memset(stats[:, :], 0.0))
        pair_free = {}
        stat_toks = []
        prev_inner = None
        prev_st_evs = []
        for p in range(DEV_PAIRS):
            par = p % 2
            wq, wk, wv = wq_t[par], wk_t[par], wv_t[par]
            wdep = wtoks[p]
            if p + 1 < 8:
                wtoks[p + 1] = load_pair_w(p + 1, pair_free.get(p - 1, []))
            gq_ev = []
            for blk in range(4):
                bank = blk % 2
                w1 = [gq_ev[blk - 2]] if blk >= 2 else (
                    list(prev_st_evs))
                mmg = pr.emit("tensor", (lambda bk, ba: lambda te: te.matmul(
                    PB[:, ba * 512:(ba + 1) * 512], e2b[:, :],
                    gam2[:, bk * 512:(bk + 1) * 512],
                    start=True, stop=True))(blk, bank), waits=w1 + [gam_tok[p]])
                gq_ev.append(pr.emit("vector", (lambda bk, ba: lambda v: v.tensor_copy(
                    out=gqB[:, bk * 512:(bk + 1) * 512],
                    in_=PB[:, ba * 512:(ba + 1) * 512]))(blk, bank), waits=[mmg]))
            gq_done = gq_ev[-1]
            if p + 1 < 8:
                gam_tok[p + 1] = load_gam(p + 1, [mmg])
            w_q0 = [ln_done, wdep[1]]
            if prev_inner is not None:
                w_q0 += list(prev_inner)
            qmm = None
            for a in range(NCT):
                for blk in range(4):
                    qmm = pr.emit("tensor", (lambda aa, bk, w2: lambda te: te.matmul(
                        PA[:, bk * 512:(bk + 1) * 512], w2[:, aa * 128:(aa + 1) * 128],
                        xnT[:, aa * T + bk * 512:aa * T + (bk + 1) * 512],
                        start=(aa == 0), stop=(aa == NCT - 1)))(a, blk, wq),
                        waits=w_q0 if (a == 0 and blk == 0) else [])
            ev_qt = pr.emit("vector", lambda v: v.tensor_tensor(
                out=qtp[:, :], in0=PA[:, :], in1=gqB[:, :], op=OP.mult),
                waits=[qmm, gq_done])
            ev_qT = pr.emit("vector", lambda v: v.tensor_copy(
                out=qTp[:, :], in_=PA[:, :]), waits=[qmm])
            v_evs = []

            def vmm_range(t0, t1):
                for t in range(t0, t1):
                    vb = t % 2
                    vw = [wdep[3], ln_done]
                    if len(v_evs) >= 2:
                        vw.append(v_evs[-2])
                    else:
                        vw += [gq_ev[2], gq_ev[3]]
                    mm = None
                    for a in range(NCT):
                        mm = pr.emit("tensor", (lambda aa, tt, bk, w2: lambda te: te.matmul(
                            PB[:, bk * 512:bk * 512 + 128],
                            xnT[:, aa * T + tt * 128:aa * T + (tt + 1) * 128],
                            w2[:, aa * 128:(aa + 1) * 128],
                            start=(aa == 0), stop=(aa == NCT - 1)))(a, t, vb, wv),
                            waits=vw if a == 0 else [])
                    v_evs.append(pr.emit("vector", (lambda tt, bk: lambda v: v.tensor_copy(
                        out=vtok[:, tt * 128:(tt + 1) * 128],
                        in_=PB[:, bk * 512:bk * 512 + 128]))(t, vb), waits=[mm]))

            vmm_range(0, 8)
            kmm = None
            for a in range(NCT):
                for blk in range(4):
                    kmm = pr.emit("tensor", (lambda aa, bk, w2: lambda te: te.matmul(
                        PA[:, bk * 512:(bk + 1) * 512], w2[:, aa * 128:(aa + 1) * 128],
                        xnT[:, aa * T + bk * 512:aa * T + (bk + 1) * 512],
                        start=(aa == 0), stop=(aa == NCT - 1)))(a, blk, wk),
                        waits=[ev_qt, ev_qT, wdep[2]] if (a == 0 and blk == 0) else [])
            ev_kT = pr.emit("vector", lambda v: v.tensor_copy(
                out=kTp[:, :], in_=PA[:, :]), waits=[kmm])
            vmm_range(8, NT)
            v_done = v_evs[-1]
            pair_free[p] = [kmm, v_evs[-1], v_evs[-2]]

            ktr_prev = {}
            ktok_evs = {}
            for c in range(NT):
                bank = c % 2
                ktr = pr.emit("tensor", (lambda cc2, bk: lambda te: te.transpose(
                    PBF[:, bk * 1024:bk * 1024 + 128],
                    kTp[:, cc2 * 128:(cc2 + 1) * 128], identb[:, :]))(c, bank),
                    waits=[ev_kT, ktr_prev.get(c - 2)])
                e1 = pr.emit("vector", (lambda cc2, bk, pp: lambda v: v.tensor_scalar(
                    out=ktokS[:, cc2 * 128:cc2 * 128 + 64],
                    in0=PBF[:, bk * 1024:bk * 1024 + 64],
                    scalar1=gkcols[:, (2 * pp) * 16 + cc2:(2 * pp) * 16 + cc2 + 1],
                    scalar2=None, op0=OP.mult))(c, bank, p), waits=[ktr])
                e2 = pr.emit("vector", (lambda cc2, bk, pp: lambda v: v.tensor_scalar(
                    out=ktokS[:, cc2 * 128 + 64:(cc2 + 1) * 128],
                    in0=PBF[:, bk * 1024 + 64:bk * 1024 + 128],
                    scalar1=gkcols[:, (2 * pp + 1) * 16 + cc2:(2 * pp + 1) * 16 + cc2 + 1],
                    scalar2=None, op0=OP.mult))(c, bank, p), waits=[ktr])
                ktr_prev[c] = e2
                ktok_evs[c] = (e1, e2)
            for hh in range(2):
                h = 2 * p + hh
                base = 64 * hh
                bank = hh
                mz = pr.emit("vector", (lambda bs: lambda v: v.memset(
                    Msb[bs:bs + 64, :], 0.0))(base),
                    waits=(list(prev_inner) if (hh == 1 and prev_inner) else [])
                          + [getattr(cx, "_taptok", None)])
                mcast0 = pr.emit("vector", (lambda bs: lambda v: v.tensor_copy(
                    out=Mbf[bs:bs + 64, :], in_=Msb[bs:bs + 64, :]))(base), waits=[mz])
                cw = [ev_kT, cx.tok_pkvb, v_done]
                cross_last = None
                for blk in range(4):
                    cross_last = pr.emit("tensor", (lambda bk, bs, pp: lambda te: te.matmul(
                        PA[bs:bs + 64, bk * 512:(bk + 1) * 512],
                        pkvb[bs:bs + 64, pp * 64:(pp + 1) * 64],
                        qTp[bs:bs + 64, bk * 512:(bk + 1) * 512],
                        start=True, stop=False,
                        tile_position=((bs, bs) if bs else None)))(blk, base, p),
                        waits=cw if blk == 0 else [])
                bank_rd = None
                mcast = mcast0
                sv_last = cross_last
                for c in range(NT):
                    w1 = [ev_qt, ev_kT] + ([bank_rd] if bank_rd else list(prev_st_evs))
                    mmst = pr.emit("tensor", (lambda cc2, bs, bk: lambda te: te.matmul(
                        PB[:, bk * 512:bk * 512 + 128],
                        kTp[bs:bs + 64, cc2 * 128:(cc2 + 1) * 128],
                        qtp[bs:bs + 64, cc2 * 128:(cc2 + 1) * 128],
                        start=True, stop=True))(c, base, bank), waits=w1)
                    ev = pr.emit("vector", (lambda cc2, hq, bk, sl: lambda v: v.scalar_tensor_tensor(
                        out=st_ch[:, sl * 128:(sl + 1) * 128],
                        in0=PB[:, bk * 512:bk * 512 + 128],
                        scalar=gkcols[:, hq * 16 + cc2:hq * 16 + cc2 + 1],
                        in1=dmasks[:, 0:128],
                        op0=OP.mult, op1=OP.mult))(c, h, bank, (2 * c + hh) % 4),
                        waits=[mmst, sv_last])
                    sv_last = pr.emit("tensor", (lambda cc2, bs, sl: lambda te: te.matmul(
                        PA[bs:bs + 64, cc2 * 128:(cc2 + 1) * 128],
                        vtok[:, cc2 * 128 + bs:cc2 * 128 + bs + 64],
                        st_ch[:, sl * 128:(sl + 1) * 128],
                        start=False, stop=False,
                        tile_position=(0, bs)))(c, base, (2 * c + hh) % 4),
                        waits=[ev, v_done])
                    sv_last = pr.emit("tensor", (lambda cc2, bs: lambda te: te.matmul(
                        PA[bs:bs + 64, cc2 * 128:(cc2 + 1) * 128],
                        Mbf[bs:bs + 64, :],
                        qtp[bs:bs + 64, cc2 * 128:(cc2 + 1) * 128],
                        start=False, stop=(cc2 % 4 == 3),
                        tile_position=((bs, bs) if bs else None)))(c, base),
                        waits=[mcast])
                    if c < NT - 1:
                        mup = pr.emit("tensor", (lambda cc2, bs, bk: lambda te: te.matmul(
                            PB[:, bk * 512 + 256:bk * 512 + 320],
                            ktokS[:, cc2 * 128:(cc2 + 1) * 128],
                            vtok[:, cc2 * 128 + bs:cc2 * 128 + bs + 64],
                            start=True, stop=True))(c, base, bank),
                            waits=[ktok_evs[c][1], ev])
                        madd = pr.emit("vector", (lambda bs, bk: lambda v: v.tensor_tensor(
                            out=Msb[bs:bs + 64, :], in0=Msb[bs:bs + 64, :],
                            in1=PB[bs:bs + 64, bk * 512 + 256:bk * 512 + 320],
                            op=OP.add))(base, bank), waits=[mup, mcast])
                        mcast = pr.emit("vector", (lambda bs: lambda v: v.tensor_copy(
                            out=Mbf[bs:bs + 64, :], in_=Msb[bs:bs + 64, :]))(base),
                            waits=[madd])
                        bank_rd = madd
                    else:
                        bank_rd = ev
                ev_r = pr.emit("vector", (lambda hq, bs: lambda v: v.tensor_scalar(
                    out=rT[bs:bs + 64, (hq // 2) * T:(hq // 2 + 1) * T],
                    in0=PA[bs:bs + 64, :], scalar1=1.0, scalar2=None,
                    op0=OP.mult, op1=OP.add,
                    accum_out=stats[bs:bs + 64, hq:hq + 1]))(h, base),
                    waits=[sv_last, stats0])
                ev_sq = pr.emit("vector", (lambda hq, bs: lambda v: v.scalar_tensor_tensor(
                    out=dump[bs:bs + 64, :],
                    in0=rT[bs:bs + 64, (hq // 2) * T:(hq // 2 + 1) * T], scalar=1.0,
                    in1=rT[bs:bs + 64, (hq // 2) * T:(hq // 2 + 1) * T],
                    op0=OP.mult, op1=OP.mult,
                    accum_out=stats[bs:bs + 64, 16 + hq:16 + hq + 1]))(h, base),
                    waits=[ev_r])
                prev_inner = (ev_r, ev_sq)
                prev_st_evs = [bank_rd]
                stat_toks += [ev_r, ev_sq]
                if tap_on(cx, b, "t_mbf5") and h == 5:
                    cx._taptok = pr.emit("sync", lambda s: s.dma_start(
                        out=cx.taps["t_mbf5"], in_=Msb[:, :]), waits=[ev_sq, bank_rd, mcast], dma=True)
                    cx.store_toks.append(cx._taptok)

        if tap_on(cx, b, "t_mbf"):
            cx.store_toks.append(pr.emit("sync", lambda s: s.dma_start(
                out=cx.taps["t_mbf"], in_=Msb[:, :]), waits=stat_toks[-2:], dma=True))
        if tap_on(cx, b, "t_ktok"):
            cx.store_toks.append(pr.emit("sync", lambda s: s.dma_start(
                out=cx.taps["t_ktok"], in_=ktokS[:, :]), waits=stat_toks[-2:], dma=True))
        if tap_on(cx, b, "t_rT"):
            cx.store_toks.append(pr.emit("sync", lambda s: s.dma_start(
                out=cx.taps["t_rT"], in_=rT[:, :]), waits=stat_toks[-2:], dma=True))

        gn_mm = pr.emit("tensor", lambda te: te.matmul(
            PB[0:1, 0:32], cx.onescol_f[:, :], stats[:, :], start=True, stop=True),
            waits=stat_toks[-2:] + list(prev_st_evs))
        rsum = pr.emit("vector", lambda v: v.reduce_sum(
            out=rows[0:1, 0:1], in_=PB[0:1, 0:16], axis=AX.X), waits=[gn_mm])
        sqsum = pr.emit("vector", lambda v: v.reduce_sum(
            out=rows[0:1, 1:2], in_=PB[0:1, 16:32], axis=AX.X), waits=[gn_mm])
        NTOT = float(NH * HD * T)
        mu_t = pr.emit("vector", lambda v: v.tensor_scalar(
            out=rows[0:1, 2:3], in0=rows[0:1, 0:1], scalar1=1.0 / NTOT, scalar2=None,
            op0=OP.mult), waits=[rsum])
        e2_t = pr.emit("vector", lambda v: v.tensor_scalar(
            out=rows[0:1, 3:4], in0=rows[0:1, 1:2], scalar1=1.0 / NTOT, scalar2=None,
            op0=OP.mult), waits=[sqsum])
        mu2_t = pr.emit("vector", lambda v: v.tensor_tensor(
            out=rows[0:1, 4:5], in0=rows[0:1, 2:3], in1=rows[0:1, 2:3], op=OP.mult),
            waits=[mu_t])
        var_t = pr.emit("vector", lambda v: v.tensor_tensor(
            out=rows[0:1, 5:6], in0=rows[0:1, 3:4], in1=rows[0:1, 4:5], op=OP.subtract),
            waits=[e2_t, mu2_t])
        ln_t = pr.emit("scalar", lambda sc: sc.activation(
            out=rows[0:1, 6:7], in_=rows[0:1, 5:6], func=AF.Ln,
            bias=cx.ccols[0:1, 0:1], scale=1.0), waits=[var_t])
        inv_t = pr.emit("scalar", lambda sc: sc.activation(
            out=rows[0:1, 7:8], in_=rows[0:1, 6:7], func=AF.Exp, scale=-0.5),
            waits=[ln_t])
        bc1 = pr.emit("tensor", lambda te: te.matmul(
            PB[:, 512:513], cx.ones1f[:, :], rows[0:1, 2:3], start=True, stop=True),
            waits=[mu_t, rsum, sqsum])
        bc2 = pr.emit("tensor", lambda te: te.matmul(
            PB[:, 513:514], cx.ones1f[:, :], rows[0:1, 7:8], start=True, stop=True),
            waits=[inv_t])
        muinv = pr.emit("vector", lambda v: v.tensor_copy(
            out=sm8[:, 32:34], in_=PB[:, 512:514]), waits=[bc1, bc2])
        g_mm = pr.emit("tensor", lambda te: te.matmul(
            PB[:, 0:8], cx.e2f[:, :], cx.gnw2s[:, :], start=True, stop=True),
            waits=[rsum, sqsum])
        g_mm2 = pr.emit("tensor", lambda te: te.matmul(
            PB[:, 8:16], cx.e2f[:, :], cx.gnb2s[:, :], start=True, stop=True),
            waits=[g_mm])
        g_ev = pr.emit("vector", lambda v: v.tensor_copy(
            out=sm8[:, 0:16], in_=PB[:, 0:16]), waits=[g_mm2])
        scl = pr.emit("vector", lambda v: v.tensor_scalar(
            out=sm8[:, 16:24], in0=sm8[:, 0:8], scalar1=sm8[:, 33:34], scalar2=None,
            op0=OP.mult), waits=[g_ev, muinv])
        tm8 = pr.emit("vector", lambda v: v.tensor_scalar(
            out=sm8[:, 24:32], in0=sm8[:, 16:24], scalar1=sm8[:, 32:33], scalar2=None,
            op0=OP.mult), waits=[scl])
        bi8 = pr.emit("vector", lambda v: v.tensor_tensor(
            out=sm8[:, 24:32], in0=sm8[:, 8:16], in1=sm8[:, 24:32], op=OP.subtract),
            waits=[tm8])
        prev_a2 = None
        for tau in range(NCT):
            a1 = pr.emit("vector", (lambda ta: lambda v: v.tensor_scalar(
                out=dump[:, :], in0=rT[:, ta * T:(ta + 1) * T],
                scalar1=sm8[:, 16 + ta:17 + ta], scalar2=sm8[:, 24 + ta:25 + ta],
                op0=OP.mult, op1=OP.add))(tau),
                waits=[scl, bi8] if tau == 0 else [prev_a2])
            prev_a2 = pr.emit("vector", (lambda ta: lambda v: v.tensor_tensor(
                out=xT[:, ta * T:(ta + 1) * T], in0=xT[:, ta * T:(ta + 1) * T],
                in1=dump[:, :], op=OP.add))(tau), waits=[a1])


def emit_ffn(cx, b, ln2_done):
    nc, pr, d = cx.nc, cx.pr, cx.d
    xT, xnT = cx.xT, cx.xnT
    PA, PB, PBF = cx.PA, cx.PB, cx.PBF
    identf, identb = cx.identf, cx.identb
    ones1f, ccols, wo_hat = cx.ones1f, cx.ccols, cx.wo_hat
    with ExitStack() as ph:
        pe = ph.enter_context
        m_all = pe(nc.sbuf_tensor(f"mall{b}", [128, NT * 80], BF16))
        g_gated = pe(nc.sbuf_tensor(f"ggt{b}", [128, NT * 80], F32))
        q5tok = pe(nc.sbuf_tensor(f"q5t{b}", [128, NT * 320], BF16))
        gacc = [pe(nc.sbuf_tensor(f"gac{b}{i}", [128, NT * 64], F32)) for i in range(2)]
        qf_tok = pe(nc.sbuf_tensor(f"qft{b}", [128, NT * 64], BF16))
        kf_tok = pe(nc.sbuf_tensor(f"kft{b}", [128, NT * 64], BF16))
        q_fT = pe(nc.sbuf_tensor(f"qfT{b}", [128, T], BF16))
        k_fT = pe(nc.sbuf_tensor(f"kfT{b}", [128, T], BF16))
        outTb = pe(nc.sbuf_tensor(f"oTb{b}", [128, T], BF16))
        linv = pe(nc.sbuf_tensor(f"linv{b}", [128, NT], F32))
        mdt = pe(nc.sbuf_tensor(f"mdt{b}", [128, NT * EF], F32))
        otok = pe(nc.sbuf_tensor(f"otok{b}", [128, 2 * 65], BF16))
        z_tok = pe(nc.sbuf_tensor(f"ztk{b}", [128, 2 * 320], BF16))
        smc = pe(nc.sbuf_tensor(f"smc{b}", [128, 8], F32))
        rowc = pe(nc.sbuf_tensor(f"rowc{b}", [1, 8], F32))
        wbuf = [pe(nc.sbuf_tensor(f"wb{b}{i}", [128, 2560], BF16)) for i in range(2)]
        gstack = ExitStack()
        g_a = gstack.enter_context(nc.sbuf_tensor(f"ga{b}", [128, NT * 80], F32))
        g_b = gstack.enter_context(nc.sbuf_tensor(f"gb{b}", [128, NT * 80], F32))
        csB = gstack.enter_context(nc.sbuf_tensor(f"csB{b}", [128, 80], F32))
        g2stack = ExitStack()
        wsdf = g2stack.enter_context(nc.sbuf_tensor(f"wsf{b}", [128, 8 * 80], F32))
        whi = g2stack.enter_context(nc.sbuf_tensor(f"whi{b}", [128, 8 * 80], BF16))
        wlo = g2stack.enter_context(nc.sbuf_tensor(f"wlo{b}", [128, 8 * 80], BF16))
        xhi = g2stack.enter_context(nc.sbuf_tensor(f"xhi{b}", [128, 1024], BF16))
        xlo = g2stack.enter_context(nc.sbuf_tensor(f"xlo{b}", [128, 1024], BF16))

        lncols = cx.lncols
        wsd_tok = pr.emit("sync", lambda s: s.dma_start(
            out=wsdf.ap().rearrange("p (a j) -> p a j", a=8),
            in_=d["Wswd"].rearrange("(a p) j -> p a j", p=128)), dma=True)
        whi_t = pr.emit("vector", lambda v: v.tensor_copy(
            out=whi[:, :], in_=wsdf[:, :]), waits=[wsd_tok])
        wlo_t = pr.emit("vector", lambda v: v.scalar_tensor_tensor(
            out=wlo[:, :], in0=whi[:, :], scalar=-1.0, in1=wsdf[:, :],
            op0=OP.mult, op1=OP.add), waits=[whi_t])
        cs_mm = None
        for a in range(NCT):
            cs_mm = pr.emit("tensor", (lambda aa: lambda te: te.matmul(
                PB[0:1, 256:336], cx.onescol_f[:, :], wsdf[:, aa * 80:(aa + 1) * 80],
                start=(aa == 0), stop=(aa == NCT - 1)))(a),
                waits=[wsd_tok, cx.lncols_tok] if a == 0 else [])
        cs_ev = pr.emit("vector", lambda v: v.tensor_copy(
            out=csB[0:1, 0:80], in_=PB[0:1, 256:336]), waits=[cs_mm])
        csb_mm = pr.emit("tensor", lambda te: te.matmul(
            PB[:, 512:592], ones1f[:, :], csB[0:1, 0:80], start=True, stop=True),
            waits=[cs_ev])
        csb_ev = pr.emit("vector", lambda v: v.tensor_copy(
            out=csB[:, :], in_=PB[:, 512:592]), waits=[csb_mm])
        umm_last = None
        uev = {}
        for w in range(4):
            for a in range(NCT):
                hi_t = pr.emit("scalar", (lambda aa, w2: lambda sc: sc.copy(
                    out=xhi[:, 0:512],
                    in_=xT[:, aa * T + w2 * 512:aa * T + (w2 + 1) * 512]))(a, w),
                    waits=[ln2_done] + ([umm_last] if umm_last is not None else []))
                lo_t = pr.emit("vector", (lambda aa, w2: lambda v: v.scalar_tensor_tensor(
                    out=xlo[:, 0:512], in0=xhi[:, 0:512], scalar=-1.0,
                    in1=xT[:, aa * T + w2 * 512:aa * T + (w2 + 1) * 512],
                    op0=OP.mult, op1=OP.add))(a, w), waits=[hi_t])
                for t4 in range(4):
                    t = w * 4 + t4
                    for ci, (xb, wb, wtk) in enumerate(
                            ((xhi, whi, whi_t), (xlo, whi, whi_t), (xhi, wlo, wlo_t))):
                        wl = []
                        if t4 == 0 and ci == 0:
                            wl = [hi_t, lo_t]
                            if a == 0:
                                wl += [wtk, csb_ev] + [uev[t - 4 + i] for i in range(4) if t - 4 + i in uev]
                        umm_last = pr.emit("tensor", (lambda aa, t4_, x2, w2: lambda te: te.matmul(
                            PA[:, t4_ * 512:t4_ * 512 + 80],
                            x2[:, t4_ * 128:(t4_ + 1) * 128], w2[:, aa * 80:(aa + 1) * 80],
                            start=(aa == 0 and x2 is xhi and w2 is whi),
                            stop=(aa == NCT - 1 and w2 is wlo)))(a, t4, xb, wb),
                            waits=wl)
            for t4 in range(4):
                t = w * 4 + t4
                uev[t] = pr.emit("vector", (lambda tt, t4_: lambda v: v.tensor_scalar(
                    out=g_b[:, tt * 80:(tt + 1) * 80], in0=PA[:, t4_ * 512:t4_ * 512 + 80],
                    scalar1=lncols[:, 16 + tt:17 + tt], scalar2=None, op0=OP.mult))(t, t4),
                    waits=[umm_last, cx.lncols_tok])
        sig_toks = {}
        exp_toks = {}
        pa_reader_tok = uev[NT - 1]
        for t in range(NT):
            t2 = pr.emit("vector", (lambda tt: lambda v: v.scalar_tensor_tensor(
                out=g_a[:, tt * 80:(tt + 1) * 80], in0=csB[:, :],
                scalar=lncols[:, 32 + tt:33 + tt],
                in1=g_b[:, tt * 80:(tt + 1) * 80], op0=OP.mult, op1=OP.subtract))(t),
                waits=[uev[t], csb_ev])
            ex = pr.emit("scalar", (lambda tt: lambda sc: sc.activation(
                out=g_b[:, tt * 80:(tt + 1) * 80], in_=g_a[:, tt * 80:(tt + 1) * 80],
                func=AF.Exp, scale=1.0))(t), waits=[t2])
            exp_toks[t] = ex
            t3 = pr.emit("vector", (lambda tt: lambda v: v.tensor_scalar(
                out=g_b[:, tt * 80:(tt + 1) * 80], in0=g_b[:, tt * 80:(tt + 1) * 80],
                scalar1=1.0, scalar2=None, op0=OP.add))(t), waits=[ex])
            sig_toks[t] = pr.emit("vector", (lambda tt: lambda v: v.reciprocal(
                out=g_a[:, tt * 80:(tt + 1) * 80], in_=g_b[:, tt * 80:(tt + 1) * 80]))(t),
                waits=[t3])
        g2stack.close()
        g_c = gstack.enter_context(nc.sbuf_tensor(f"gc{b}", [128, NT * 80], F32))
        cnt0 = pr.emit("vector", lambda v: v.memset(g_c[:, :], 0.0),
                       waits=[umm_last, sig_toks[NT - 1]])
        cnt_toks = {t: cnt0 for t in range(NT)}
        for dl in range(1, EF):
            for t in range(NT):
                def gv(buf, tt, lo, dd):
                    v3 = buf.ap().rearrange("p (tt g e3) -> p tt g e3", tt=NT, e3=EF)
                    return v3[:, tt:tt + 1, :, lo:lo + (EF - dd)]
                c1 = pr.emit("vector", (lambda tt, dd: lambda v: v.tensor_tensor(
                    out=gv(g_b, tt, 0, dd), in0=gv(g_a, tt, 0, dd),
                    in1=gv(g_a, tt, dd, dd), op=OP.is_ge))(t, dl),
                    waits=[sig_toks[t], cnt_toks[t]])
                a1 = pr.emit("vector", (lambda tt, dd: lambda v: v.tensor_tensor(
                    out=gv(g_c, tt, dd, dd), in0=gv(g_c, tt, dd, dd),
                    in1=gv(g_b, tt, 0, dd), op=OP.add))(t, dl), waits=[c1])
                c2 = pr.emit("vector", (lambda tt, dd: lambda v: v.tensor_tensor(
                    out=gv(g_b, tt, dd, dd), in0=gv(g_a, tt, dd, dd),
                    in1=gv(g_a, tt, 0, dd), op=OP.is_gt))(t, dl), waits=[a1])
                a2 = pr.emit("vector", (lambda tt, dd: lambda v: v.tensor_tensor(
                    out=gv(g_c, tt, 0, dd), in0=gv(g_c, tt, 0, dd),
                    in1=gv(g_b, tt, dd, dd), op=OP.add))(t, dl), waits=[c2])
                cnt_toks[t] = a2
        msk_toks = {}
        gat_toks = {}
        for t in range(NT):
            m1 = pr.emit("vector", (lambda tt: lambda v: v.tensor_scalar(
                out=m_all[:, tt * 80:(tt + 1) * 80], in0=g_c[:, tt * 80:(tt + 1) * 80],
                scalar1=3.0, scalar2=None, op0=OP.is_lt))(t), waits=[cnt_toks[t]])
            msk_toks[t] = m1
            gat_toks[t] = pr.emit("vector", (lambda tt: lambda v: v.tensor_tensor(
                out=g_gated[:, tt * 80:(tt + 1) * 80], in0=g_a[:, tt * 80:(tt + 1) * 80],
                in1=m_all[:, tt * 80:(tt + 1) * 80], op=OP.mult))(t), waits=[m1])
        if tap_on(cx, b, "t_gates"):
            cx.store_toks.append(pr.emit("sync", lambda s: s.dma_start(
                out=cx.taps["t_gates"], in_=g_gated[:, :]),
                waits=[gat_toks[NT - 1]], dma=True))
        if tap_on(cx, b, "t_masks"):
            cx.store_toks.append(pr.emit("sync", lambda s: s.dma_start(
                out=cx.taps["t_masks"], in_=m_all[:, :]),
                waits=[msk_toks[NT - 1], gat_toks[NT - 1]], dma=True))
        gstack.close()
        ET = pe(nc.sbuf_tensor(f"ETb{b}", [128, 8 * 512], BF16))
        cx._et_rd = {}
        vaug = pe(nc.sbuf_tensor(f"vag{b}", [128, NT * 128], BF16))
        pr.emit("vector", lambda v: v.memset(q_fT[64:128, :], 0.0))
        pad_tok = pr.emit("vector", lambda v: v.memset(k_fT[64:128, :], 0.0))
        cx._pad_tok = pad_tok
        vz = pr.emit("vector", lambda v: v.memset(vaug[:, :], 0.0))
        vaug_ones = pr.emit("vector", lambda v: v.memset(
            vaug.ap().rearrange("p (tt j) -> p tt j", j=128)[:, :, 64:65], 1.0),
            waits=[vz])
        zp = cx.zp

        lpark = pe(nc.sbuf_tensor(f"lpark{b}", [128, 1024], F32))

        def wsrc(h, kind):
            if kind == 0:
                return d["Wq"].rearrange("(a p) j -> p a j", p=128)[:, :, h * 320:(h + 1) * 320]
            if kind == 1:
                return d["Wkv"].rearrange("(a p) j -> p a j", p=128)[:, :, h * 320:(h + 1) * 320]
            return d["Wkv"].rearrange("(a p) j -> p a j", p=128)[
                :, :, 2560 + h * 320:2560 + (h + 1) * 320]

        wload = {}
        proj_done = {}

        def sched_load(i):
            h2, k2 = i // 3, i % 3
            wload[i] = pr.emit("gpsimd", (lambda s_, pa: lambda g: g.dma_start(
                out=wbuf[pa].ap().rearrange("p (a j) -> p a j", a=8), in_=s_))(
                    wsrc(h2, k2), i % 2),
                waits=[proj_done[i - 2]] if i - 2 in proj_done else [], dma=True)

        sched_load(0)
        sched_load(1)
        st_ = {
            "pa01": [pa_reader_tok],
            "pa23": [pa_reader_tok],
            "pb": [cs_ev, csb_ev],
            "pbf": [None, None],
        }

        def pbf_wr(bank, fn, waits):
            tok = pr.emit("tensor", fn, waits=waits + [st_["pbf"][bank]])
            return tok

        def pbf_rd(bank, eng, fn, waits):
            tok = pr.emit(eng, fn, waits=waits)
            st_["pbf"][bank] = tok
            return tok

        def gen_proj(h):
            kd = {}
            for kind in range(3):
                i = h * 3 + kind
                par = i % 2
                ev_pj = {}
                mmq = None
                for t in range(NT):
                    bank = 2 + (t % 2)
                    wl = []
                    if t == 0:
                        wl = [wload[i], ln2_done] + st_["pa23"]
                    if t >= 2:
                        wl.append(ev_pj[t - 2])
                    for a in range(NCT):
                        mmq = pr.emit("tensor", (lambda aa, tt, bk, pa2: lambda te: te.matmul(
                            PA[:, bk * 512:bk * 512 + 320],
                            xnT[:, aa * T + tt * 128:aa * T + (tt + 1) * 128],
                            wbuf[pa2][:, aa * 320:(aa + 1) * 320],
                            start=(aa == 0), stop=(aa == NCT - 1)))(a, t, bank, par),
                            waits=wl if a == 0 else [])
                    ev_pj[t] = pr.emit("scalar", (lambda tt, bk: lambda sc: sc.copy(
                        out=q5tok[:, tt * 320:(tt + 1) * 320],
                        in_=PA[:, bk * 512:bk * 512 + 320]))(t, bank),
                        waits=[mmq, cx._prev_gd.get(t)])
                    if t == 7:
                        yield
                proj_done[i] = mmq
                st_["pa23"] = [ev_pj[NT - 2], ev_pj[NT - 1]]
                if i + 2 < 24 and (i + 2) not in wload:
                    sched_load(i + 2)
                yield
                coff = (40 if kind == 0 else 0) + h * EF
                gdone = {}
                for ee in range(EF):
                    for t in range(NT):
                        if ee == 0:
                            gdone[t] = pr.emit("vector", (lambda tt, e3, c0: lambda v: v.tensor_scalar(
                                out=gacc[0][:, tt * 64:(tt + 1) * 64],
                                in0=q5tok[:, tt * 320 + e3 * 64:tt * 320 + (e3 + 1) * 64],
                                scalar1=g_gated[:, tt * 80 + c0 + e3:tt * 80 + c0 + e3 + 1],
                                scalar2=None, op0=OP.mult))(t, ee, coff),
                                waits=[ev_pj[t], gat_toks[t]])
                        else:
                            if ee == EF - 1 and kind == 2:
                                oap = (lambda tt: vaug[:, tt * 128:tt * 128 + 64])
                            elif ee == EF - 1 and kind == 1:
                                oap = (lambda tt: kf_tok[:, tt * 64:(tt + 1) * 64])
                            elif ee == EF - 1:
                                oap = (lambda tt: qf_tok[:, tt * 64:(tt + 1) * 64])
                            else:
                                oap = (lambda tt, e3=ee: gacc[e3 % 2][:, tt * 64:(tt + 1) * 64])
                            gdone[t] = pr.emit("vector", (lambda tt, e3, c0, oa: lambda v: v.scalar_tensor_tensor(
                                out=oa(tt),
                                in0=q5tok[:, tt * 320 + e3 * 64:tt * 320 + (e3 + 1) * 64],
                                scalar=g_gated[:, tt * 80 + c0 + e3:tt * 80 + c0 + e3 + 1],
                                in1=gacc[(e3 + 1) % 2][:, tt * 64:(tt + 1) * 64],
                                op0=OP.mult, op1=OP.add))(t, ee, coff, oap), waits=[gdone[t]])
                kd[kind] = gdone
                cx._prev_gd = gdone
                yield
            for kind in range(2):
                dst = q_fT if kind == 0 else k_fT
                tpv = {}
                for t in range(NT):
                    bank = t % 2
                    tpq = pbf_wr(bank, (lambda tt, bk, kk: lambda te: te.transpose(
                        PBF[0:64, bk * 1024:bk * 1024 + 128],
                        (qf_tok if kk == 0 else kf_tok)[:, tt * 64:(tt + 1) * 64],
                        identb[:, :]))(t, bank, kind),
                        [kd[kind][t]])
                    tpv[t] = pbf_rd(bank, "vector", (lambda tt, bk, d2: lambda v: v.tensor_copy(
                        out=d2[0:64, tt * 128:(tt + 1) * 128],
                        in_=PBF[0:64, bk * 1024:bk * 1024 + 128]))(t, bank, dst),
                        [tpq])
                    if t == 7:
                        yield
                kd[10 + kind] = tpv
                yield
            qT_done = kd[10][NT - 1]
            kT_done = kd[11][NT - 1]
            po = (h % 2) * 4
            r1 = pr.emit("vector", (lambda p_: lambda v: v.tensor_reduce(
                out=smc[0:64, p_:p_ + 1], in_=q_fT[0:64, :], axis=AX.X, op=OP.max,
                apply_absolute_value=True))(po), waits=[qT_done])
            r2 = pr.emit("vector", (lambda p_: lambda v: v.tensor_reduce(
                out=smc[0:64, p_ + 1:p_ + 2], in_=k_fT[0:64, :], axis=AX.X, op=OP.max,
                apply_absolute_value=True))(po), waits=[kT_done])
            cx._ffn_rr[h] = (r1, r2)
            cx._ffn_kd[h] = kd

        def gen_attn(h):
            kd = cx._ffn_kd[h]
            v_done = kd[2][NT - 1]
            r1, r2 = cx._ffn_rr[h]
            po = (h % 2) * 4
            tq = pr.emit("tensor", (lambda p_: lambda te: te.matmul(
                PB[0:1, 0:64], smc[0:64, p_:p_ + 1], identf[0:64, 0:64],
                start=True, stop=True))(po), waits=[r1] + st_["pb"])
            tk = pr.emit("tensor", (lambda p_: lambda te: te.matmul(
                PB[0:1, 64:128], smc[0:64, p_ + 1:p_ + 2], identf[0:64, 0:64],
                start=True, stop=True))(po), waits=[r2])
            mq = pr.emit("vector", (lambda p_: lambda v: v.reduce_max(
                out=rowc[0:1, p_:p_ + 1], in_=PB[0:1, 0:64], axis=AX.X))(po),
                waits=[tq, tk])
            mk = pr.emit("vector", (lambda p_: lambda v: v.reduce_max(
                out=rowc[0:1, p_ + 1:p_ + 2], in_=PB[0:1, 64:128], axis=AX.X))(po),
                waits=[tq, tk])
            cc1 = pr.emit("vector", (lambda p_: lambda v: v.tensor_tensor(
                out=rowc[0:1, p_ + 2:p_ + 3], in0=rowc[0:1, p_:p_ + 1],
                in1=rowc[0:1, p_ + 1:p_ + 2], op=OP.mult))(po), waits=[mq, mk])
            ncc = pr.emit("vector", (lambda p_: lambda v: v.tensor_scalar(
                out=rowc[0:1, p_ + 3:p_ + 4], in0=rowc[0:1, p_ + 2:p_ + 3],
                scalar1=-1.0, scalar2=None, op0=OP.mult))(po), waits=[cc1])
            bcm = pr.emit("tensor", (lambda p_: lambda te: te.matmul(
                PB[:, 200:201], ones1f[:, :], rowc[0:1, p_ + 3:p_ + 4],
                start=True, stop=True))(po), waits=[ncc, mq, mk])
            negC = pr.emit("vector", (lambda p_: lambda v: v.tensor_copy(
                out=smc[:, p_ + 2:p_ + 3], in_=PB[:, 200:201]))(po), waits=[bcm])
            ex_prev = {}
            lc_mms = []
            lin_half = None

            def emit_lg(ib, j, extra=()):
                pp = j % 2
                sl = j % 8
                wl = list(extra)
                if j >= 2:
                    wl.append(ex_prev[j - 2])
                elif (NT - 2 + j) in ex_prev:
                    wl.append(ex_prev[NT - 2 + j])
                lg = pr.emit("tensor", (lambda jj, i2, p2: lambda te: te.matmul(
                    PB[:, p2 * 512:(p2 + 1) * 512],
                    k_fT[:, jj * 128:(jj + 1) * 128],
                    q_fT[:, i2 * 512:(i2 + 1) * 512],
                    start=True, stop=True))(j, ib, pp),
                    waits=wl + [cx._pad_tok])
                ex_prev[j] = pr.emit("scalar", (lambda jj, s2, p2, p_: lambda sc: sc.activation(
                    out=ET[:, s2 * 512:(s2 + 1) * 512],
                    in_=PB[:, p2 * 512:(p2 + 1) * 512],
                    func=AF.Exp, bias=smc[:, p_ + 2:p_ + 3], scale=0.125))(j, sl, pp, (h % 2) * 4),
                    waits=[lg, negC, cx._et_rd.get(sl)])

            def emit_pv(ib, j, extra=()):
                bank = ib % 2
                sl = j % 8
                tok = pr.emit("tensor", (lambda jj, s2, i2, bk: lambda te: te.matmul(
                    PA[:, bk * 512:(bk + 1) * 512], vaug[:, jj * 128:(jj + 1) * 128],
                    ET[:, s2 * 512:(s2 + 1) * 512],
                    start=(jj == 0), stop=(jj == NT - 1)))(j, sl, ib, bank),
                    waits=[ex_prev[j], vaug_ones] + list(extra))
                cx._et_rd[sl] = tok
                return tok

            for half in range(2):
                pvA = pvB = None
                extra0 = ([negC, v_done] + st_["pa01"]) if half == 0 else [ot_ev, l_ev]
                emit_lg(2 * half, 0, [negC] + (st_["pb"] if half == 0 else ([lin_half] if lin_half else [])))
                for j in range(NT):
                    if j + 1 < NT:
                        emit_lg(2 * half, j + 1)
                    pvA = emit_pv(2 * half, j, extra0 if j == 0 else ())
                yield
                emit_lg(2 * half + 1, 0, [lin_half] if lin_half else [])
                for j in range(NT):
                    if j + 1 < NT:
                        emit_lg(2 * half + 1, j + 1)
                    pvB = emit_pv(2 * half + 1, j, extra0 if j == 0 else ())
                ot_ev = pr.emit("vector", (lambda hf: lambda v: v.tensor_copy(
                    out=outTb[0:65, hf * 1024:(hf + 1) * 1024], in_=PA[0:65, 0:1024]))(half),
                    waits=[pvA, pvB])
                l_ev = pr.emit("vector", lambda v: v.tensor_copy(
                    out=lpark[64:65, 0:1024], in_=PA[64:65, 0:1024]),
                    waits=[pvA, pvB] + (lc_mms[-1:] if lc_mms else []))
                for t8 in range(8):
                    lc_mms.append(pr.emit("tensor", (lambda t8_: lambda te: te.matmul(
                        PB[:, 256 + t8_:256 + t8_ + 1],
                        lpark[64:65, t8_ * 128:(t8_ + 1) * 128],
                        ccols[64:65, 1:2], start=True, stop=True))(t8),
                        waits=[l_ev] + ([lin_half] if half else [])))
                lin_half = pr.emit("vector", (lambda hf: lambda v: v.reciprocal(
                    out=linv[:, hf * 8:(hf + 1) * 8], in_=PB[:, 256:264]))(half),
                    waits=lc_mms[-1:])
                st_["pa01"] = [ot_ev, l_ev]
                st_["pb"] = st_["pb"] + [lin_half]
                yield
            lin = lin_half
            st_["pb"] = [lin]
            md_toks = {}
            for t in range(NT):
                md_toks[t] = pr.emit("vector", (lambda tt, h2: lambda v: v.tensor_scalar(
                    out=mdt[:, tt * EF:(tt + 1) * EF],
                    in0=m_all[:, tt * 80 + 40 + h2 * EF:tt * 80 + 40 + (h2 + 1) * EF],
                    scalar1=linv[:, tt:tt + 1], scalar2=None, op0=OP.mult))(t, h),
                    waits=[lin, msk_toks[t]])
            yield
            hh = h % 2
            gp_prev = {}
            oev_prev = {}
            for t in range(NT):
                bank = t % 2
                otr = pbf_wr(bank, (lambda tt, bk: lambda te: te.transpose(
                    PBF[:, bk * 1024:bk * 1024 + 65], outTb[0:65, tt * 128:(tt + 1) * 128],
                    identb[0:65, 0:65]))(t, bank), [ot_ev, gp_prev.get(t - 2)])
                oev = pbf_rd(bank, "vector", (lambda tt, bk: lambda v: v.tensor_copy(
                    out=otok[:, (tt % 2) * 65:(tt % 2) * 65 + 65],
                    in_=PBF[:, bk * 1024:bk * 1024 + 65]))(t, bank),
                    [otr, gp_prev.get(t - 2)])
                oev_prev[t] = oev
                gp_t = oev
                for ee in range(EF):
                    gp_t = pr.emit("vector", (lambda tt, e3: lambda v: v.tensor_scalar(
                        out=z_tok[:, (tt % 2) * 320 + e3 * 64:(tt % 2) * 320 + (e3 + 1) * 64],
                        in0=otok[:, (tt % 2) * 65:(tt % 2) * 65 + 64],
                        scalar1=mdt[:, tt * EF + e3:tt * EF + e3 + 1], scalar2=None,
                        op0=OP.mult))(t, ee),
                        waits=[oev, md_toks[t]] + ([ztr_prev] if ee == 0 and t >= 2 else []))
                gp_prev[t] = gp_t
                if hh == 0:
                    spans = [(0, 128, 0, 0), (128, 256, 1, 0), (256, 320, 2, 0)]
                else:
                    spans = [(0, 64, 2, 64), (64, 192, 3, 0), (192, 320, 4, 0)]
                for (lo, hi, chunk, rb) in spans:
                    wid = hi - lo
                    bank2 = 1 - bank
                    tpz = pbf_wr(bank2, (lambda tt, l2, w2, rb2, bk: lambda te: te.transpose(
                        PBF[rb2:rb2 + w2, bk * 1024 + 256:bk * 1024 + 256 + 128],
                        z_tok[:, (tt % 2) * 320 + l2:(tt % 2) * 320 + l2 + w2], identb[:, :],
                        tile_position=((0, rb2) if rb2 else None)))(t, lo, wid, rb, bank2),
                        [gp_t])
                    zev = pbf_rd(bank2, "vector", (lambda tt, ch, w2, rb2, bk: lambda v: v.tensor_copy(
                        out=zp[rb2:rb2 + w2, ch * T + tt * 128:ch * T + (tt + 1) * 128],
                        in_=PBF[rb2:rb2 + w2, bk * 1024 + 256:bk * 1024 + 256 + 128]))(
                            t, chunk, wid, rb, bank2), [tpz])
                    ztr_prev = zev
                cx._z_evs.append(zev)
                if t % 4 == 3:
                    yield
            if hh == 1:
                first_wl = cx._z_evs[-3:] + [cx.tok_wo] + st_["pa01"]
                wo_evs = []
                for m in range(NCT):
                    for half in range(2):
                        mm = None
                        wl = list(first_wl) if (m == 0 and half == 0) else wo_evs[-2:]
                        for k in range(5):
                            for tb2 in range(2):
                                mm = pr.emit("tensor", (lambda kk, m2, hf, t2: lambda te: te.matmul(
                                    PA[:, t2 * 512:(t2 + 1) * 512],
                                    wo_hat[:, kk * 1024 + m2 * 128:kk * 1024 + (m2 + 1) * 128],
                                    zp[:, kk * T + hf * 1024 + t2 * 512:kk * T + hf * 1024 + (t2 + 1) * 512],
                                    start=(kk == 0), stop=(kk == 4)))(k, m, half, tb2),
                                    waits=wl if (k == 0 and tb2 == 0) else [])
                        for tb2 in range(2):
                            wo_evs.append(pr.emit("vector", (lambda m2, hf, t2: lambda v: v.tensor_tensor(
                                out=xT[:, m2 * T + hf * 1024 + t2 * 512:m2 * T + hf * 1024 + (t2 + 1) * 512],
                                in0=xT[:, m2 * T + hf * 1024 + t2 * 512:m2 * T + hf * 1024 + (t2 + 1) * 512],
                                in1=PA[:, t2 * 512:(t2 + 1) * 512], op=OP.add))(m, half, tb2),
                                waits=[mm]))
                    if m % 2 == 1:
                        yield
                st_["pa01"] = wo_evs[-2:]
                cx._z_evs = []
                cx._ffn_done = wo_evs

        cx._ffn_kd = {}
        cx._ffn_rr = {}
        cx._prev_gd = {}
        cx._z_evs = []
        cx._ffn_done = []
        for _ in gen_proj(0):
            pass
        for h in range(HF):
            ga = gen_attn(h)
            gp2 = gen_proj(h + 1) if h + 1 < HF else None
            while True:
                a_done = p_done = False
                try:
                    next(ga)
                except StopIteration:
                    a_done = True
                if gp2 is not None:
                    try:
                        next(gp2)
                    except StopIteration:
                        p_done = True
                else:
                    p_done = True
                if a_done and p_done:
                    break
                if a_done and not p_done:
                    for _ in gp2:
                        pass
                    break
                if p_done:
                    gp2 = None
        env_done = cx._ffn_done
        Ctx_ffn = env_done
        if not hasattr(cx, "_ffn_done_map"):
            cx._ffn_done_map = {}
        cx._ffn_done_map[b] = env_done[-4:]


def kernel(L0_attn_x, L0_attn_past_kv, L0_attn_W_qkv, L0_attn_gn_weight,
           L0_attn_gn_bias, L0_ffn_Wq, L0_ffn_Wkv, L0_ffn_Ws, L0_ffn_Wd, L0_ffn_Wo):
    x = np.ascontiguousarray(np.asarray(L0_attn_x, dtype=np.float32))
    ncores = 8
    consts = make_consts()
    wo = np.asarray(L0_ffn_Wo, np.float32)
    wo_hat = np.ascontiguousarray(
        np.tile(wo.reshape(64, 5, 1024).transpose(1, 0, 2).reshape(320, 1024), (2, 1)))
    base = {
        "Wqkv": np.ascontiguousarray(np.asarray(L0_attn_W_qkv, np.float32)),
        "pkv": np.ascontiguousarray(np.asarray(L0_attn_past_kv, np.float32).reshape(8, 128, 64)),
        "gnw2": np.ascontiguousarray(np.asarray(L0_attn_gn_weight, np.float32).reshape(8, 2).T),
        "gnb2": np.ascontiguousarray(np.asarray(L0_attn_gn_bias, np.float32).reshape(8, 2).T),
        "Wq": np.ascontiguousarray(np.asarray(L0_ffn_Wq, np.float32)),
        "Wkv": np.ascontiguousarray(np.asarray(L0_ffn_Wkv, np.float32)),
        "Wswd": np.ascontiguousarray(np.concatenate(
            [np.asarray(L0_ffn_Ws, np.float32), np.asarray(L0_ffn_Wd, np.float32)], axis=1)),
        "Wo_hat": wo_hat,
    }
    base.update(consts)
    nc = build(taps=DEV_TAPS)
    in_maps = []
    for c in range(ncores):
        im = dict(base)
        im["x"] = np.ascontiguousarray(x[c * NB:(c + 1) * NB])
        in_maps.append(im)
    res = run_bass_kernel_spmd(nc, in_maps, core_ids=list(range(ncores)))
    return np.concatenate([r["y"] for r in res.results], axis=0)
